# version 01
# baseline (speedup 1.0000x reference)
import contextlib
import numpy as np
import concourse.bass as bass
import concourse.mybir as mybir
from concourse.bass_utils import run_bass_kernel_spmd

F32 = mybir.dt.float32
BF16 = mybir.dt.bfloat16
AF = mybir.ActivationFunctionType
ALU = mybir.AluOpType
AX = mybir.AxisListType

NCORES = 8
B, T, C = 16, 2048, 2048
H, DH = 16, 128
FC = C // 128
SCALE = float(DH) ** -0.5
NEG = -1.0e30
FH = 5632
HC = FH // 128
NB = 256

ENGS = ("tensor", "vector", "scalar", "gpsimd", "sync")


DMA_POOL = 8


class _Op:
    __slots__ = ("eng", "fn", "reads", "writes", "dma", "idx", "need_inc", "token",
                 "deps", "sem", "selfwait")


class Plan:

    def __init__(self, nc):
        self.nc = nc
        self.ops = []
        self.last_w = {}
        self.readers = {}
        self.dma_idx = {e: 0 for e in ENGS}

    def op(self, eng, fn, r=(), w=(), dma=False):
        o = _Op()
        o.eng, o.fn, o.reads, o.writes, o.dma = eng, fn, list(r), list(w), dma
        o.need_inc = False
        o.token = None
        o.sem = None
        o.selfwait = None
        o.idx = len(self.ops)
        if dma:
            o.sem = (eng, self.dma_idx[eng] % DMA_POOL)
            self.dma_idx[eng] += 1
        deps = {}

        def add_dep(p):
            if p is None or p is o:
                return
            if p.eng == o.eng and not p.dma and not o.dma and p.eng == "tensor":
                return
            deps[p.idx] = p

        for k in o.reads:
            add_dep(self.last_w.get(k))
        for k in o.writes:
            add_dep(self.last_w.get(k))
            for rd in self.readers.get(k, ()):
                add_dep(rd)
        o.deps = list(deps.values())
        for p in o.deps:
            p.need_inc = True
        for k in o.writes:
            self.last_w[k] = o
            self.readers[k] = []
        for k in o.reads:
            self.readers.setdefault(k, []).append(o)
        self.ops.append(o)
        return o

    def build(self, es):
        nc = self.nc
        sems = {e: es.enter_context(nc.semaphore(f"sem_{e}")) for e in ENGS}
        dsems = {}
        for e in ENGS:
            n = min(self.dma_idx[e], DMA_POOL)
            for i in range(n):
                dsems[(e, i)] = es.enter_context(nc.semaphore(f"dsem_{e}{i}"))
        counts = {e: 0 for e in ENGS}
        dcounts = {k: 0 for k in dsems}
        for o in self.ops:
            if o.dma:
                o.need_inc = True
                o.selfwait = dcounts[o.sem]
                dcounts[o.sem] += 16
                o.token = dcounts[o.sem]
            elif o.need_inc:
                counts[o.eng] += 1
                o.token = counts[o.eng]
        totals = dict(counts)
        dtotals = dict(dcounts)
        floors = {e: {} for e in ENGS}

        def semref(o):
            return dsems[o.sem] if o.dma else sems[o.eng]

        def semkey(o):
            return o.sem if o.dma else o.eng

        def emit_for(eng_name, e):
            fl = floors[eng_name]
            for o in self.ops:
                if o.eng != eng_name:
                    continue
                if o.dma and o.selfwait > fl.get(o.sem, 0):
                    e.wait_ge(dsems[o.sem], o.selfwait)
                    fl[o.sem] = o.selfwait
                for p in o.deps:
                    k = semkey(p)
                    if p.token > fl.get(k, 0):
                        e.wait_ge(semref(p), p.token)
                        fl[k] = p.token
                ins = o.fn(e)
                if o.need_inc:
                    ins.then_inc(semref(o), 16 if o.dma else 1)
            if eng_name == "sync":
                for e2 in ENGS:
                    if totals[e2] > fl.get(e2, 0):
                        e.wait_ge(sems[e2], totals[e2])
                for k, v in dtotals.items():
                    if v > fl.get(k, 0):
                        e.wait_ge(dsems[k], v)

        with nc.Block() as block:
            for e in ENGS:
                getattr(block, e)((lambda en: (lambda eng_obj: emit_for(en, eng_obj)))(e))


def keys2k(name, c0, c1):
    return [f"{name}#{i}" for i in range(c0 // 2048, (c1 + 2047) // 2048)]


class Ctx:
    pass


def alloc_common(nc, es, cx):
    cx.ps = [es.enter_context(nc.psum_tensor(f"ps{i}", [128, 512], F32)) for i in range(7)]
    cx.psl = es.enter_context(nc.psum_tensor("psl", [1, 512], F32))
    cx.ps_i = 0

    def rot():
        cx.ps_i = (cx.ps_i + 1) % 7
        return cx.ps[cx.ps_i], f"PS.{cx.ps_i}"
    cx.rot = rot
    sb = lambda n, sh, dt: es.enter_context(nc.sbuf_tensor(n, sh, dt))
    cx.ones_r = sb("ones_r", [1, 512], BF16)
    cx.ones_c = sb("ones_c", [128, 1], BF16)
    cx.ones_f = sb("ones_f", [128, 1], F32)
    cx.ones_fr = sb("ones_fr", [1, 128], F32)
    cx.negm = [sb(f"negm{i}", [128, 1], F32) for i in range(2)]
    cx.accq = [sb(f"accq{i}", [128, 1], F32) for i in range(2)]
    cx.tmp1 = [sb(f"tmp1_{i}", [128, 1], F32) for i in range(2)]
    cx.rstd = [sb(f"rstd{i}", [128, 1], F32) for i in range(2)]


def prologue_consts(P, cx):
    P.op("gpsimd", lambda e: e.memset(cx.ones_r[:, :], 1.0), w=["ones_r"])
    P.op("gpsimd", lambda e: e.memset(cx.ones_c[:, :], 1.0), w=["ones_c"])
    P.op("gpsimd", lambda e: e.memset(cx.ones_f[:, :], 1.0), w=["ones_f"])
    P.op("gpsimd", lambda e: e.memset(cx.ones_fr[:, :], 1.0), w=["ones_fr"])


def ln_tile(P, cx, xt, xt_key, xnt, xnt_key, pp):
    negm, accq, tmp1, rstd = cx.negm[pp], cx.accq[pp], cx.tmp1[pp], cx.rstd[pp]
    kn, ka, kt, kr = f"negm{pp}", f"accq{pp}", f"tmp1{pp}", f"rstd{pp}"
    grp, gk = cx.rot()
    P.op("vector", lambda e: e.reduce_sum(negm[:, :], xt[:, :], axis=AX.X, negate=True),
         r=[xt_key], w=[kn])
    P.op("vector", lambda e: e.tensor_scalar(negm[:, :], negm[:, :], 1.0 / C, None, op0=ALU.mult),
         r=[kn], w=[kn])
    P.op("scalar", lambda e: e.activation(grp[:, :], xt[:, 0:512], AF.Square, accum_out=accq[:, :]),
         r=[xt_key], w=[gk, ka])
    for j in range(1, 4):
        P.op("scalar", lambda e, j=j: e.activation(grp[:, :], xt[:, j * 512:(j + 1) * 512],
                                                   AF.Square, accum_out=tmp1[:, :]),
             r=[xt_key], w=[gk, kt])
        P.op("vector", lambda e: e.tensor_tensor(accq[:, :], accq[:, :], tmp1[:, :], op=ALU.add),
             r=[ka, kt], w=[ka])
    P.op("vector", lambda e: e.tensor_tensor(tmp1[:, :], negm[:, :], negm[:, :], op=ALU.mult),
         r=[kn], w=[kt])
    P.op("vector", lambda e: e.tensor_scalar(tmp1[:, :], tmp1[:, :], -1.0, 1e-5,
                                             op0=ALU.mult, op1=ALU.add),
         r=[kt], w=[kt])
    P.op("scalar", lambda e: e.activation(rstd[:, :], accq[:, :], AF.Sqrt, bias=tmp1[:, :],
                                          scale=1.0 / C),
         r=[ka, kt], w=[kr])
    P.op("vector", lambda e: e.reciprocal(rstd[:, :], rstd[:, :]), r=[kr], w=[kr])
    P.op("vector", lambda e: e.tensor_scalar(xnt[:, :], xt[:, :], negm[:, :], rstd[:, :],
                                             op0=ALU.add, op1=ALU.mult),
         r=[xt_key, kn, kr], w=[xnt_key])


def build_stage_a(nc, debug=False):
    es = contextlib.ExitStack()
    cx = Ctx()
    x_h = nc.dram_tensor("xa", [2, T, C], F32, kind="ExternalInput").ap()
    wq_h = nc.dram_tensor("wq", [H, C, DH], F32, kind="ExternalInput").ap()
    wk_h = nc.dram_tensor("wk", [H, C, DH], F32, kind="ExternalInput").ap()
    wv_h = nc.dram_tensor("wv", [H, C, DH], F32, kind="ExternalInput").ap()
    bq_h = nc.dram_tensor("bq", [H, DH], F32, kind="ExternalInput").ap()
    bk_h = nc.dram_tensor("bk", [H, DH], F32, kind="ExternalInput").ap()
    bv_h = nc.dram_tensor("bv", [H, DH], F32, kind="ExternalInput").ap()
    wo_h = nc.dram_tensor("wo", [C, C], F32, kind="ExternalInput").ap()
    bo_h = nc.dram_tensor("bo", [C], F32, kind="ExternalInput").ap()
    wf_h = nc.dram_tensor("wf", [C, C], F32, kind="ExternalInput").ap()
    bf_h = nc.dram_tensor("bf", [C], F32, kind="ExternalInput").ap()
    x1_h = nc.dram_tensor("x1out", [2, T, C], F32, kind="ExternalOutput").ap()

    xn_d = nc.dram_tensor("xn_d", [T, C], BF16).ap()
    bw_d = nc.dram_tensor("bw_d", [C, C], BF16).ap()
    g_d = nc.dram_tensor("g_d", [T, C], BF16).ap()
    v_d = nc.dram_tensor("v_d", [H, T, DH], BF16).ap()
    y_d = nc.dram_tensor("y_d", [H, DH, T], BF16).ap()

    alloc_common(nc, es, cx)
    sb = lambda n, sh, dt: es.enter_context(nc.sbuf_tensor(n, sh, dt))
    XNFM = sb("XNFM", [128, FC * 2048], BF16)
    PTOFF = []
    off = 0
    for c in range(16):
        PTOFF.append(off)
        off += T - c * 128
    PT = sb("PT", [128, off], BF16)
    XT = [sb(f"XT{i}", [128, C], F32) for i in range(2)]
    XNT = sb("XNT", [128, C], BF16)
    GT = [sb(f"GT{i}", [128, C], BF16) for i in range(2)]
    WVC = sb("WVC", [128, C], BF16)
    VT = sb("VT", [128, 512], BF16)
    WQH = [sb(f"WQH{i}", [128, C], BF16) for i in range(2)]
    WKH = [sb(f"WKH{i}", [128, C], BF16) for i in range(2)]
    QFM = sb("QFM", [128, T], BF16)
    KFM = sb("KFM", [128, T], BF16)
    VH = sb("VH", [128, T], BF16)
    YH = sb("YH", [128, T], BF16)
    RB = sb("RB", [128, T], BF16)
    LNL = sb("LNL", [1, T], BF16)
    SCR = sb("SCR", [128, 512], F32)
    MASKT = sb("MASKT", [128, 128], BF16)
    BQ = sb("BQ", [1, 2048], BF16)
    BK = sb("BK", [1, 2048], BF16)
    BV = sb("BV", [1, 2048], BF16)
    BO = sb("BO", [1, 2048], BF16)
    BF_ = sb("BF_", [1, 2048], BF16)

    P = Plan(nc)
    prologue_consts(P, cx)
    for (row, src) in ((BQ, bq_h), (BK, bk_h), (BV, bv_h)):
        P.op("gpsimd", lambda e, row=row, src=src: e.dma_start(
            out=row[0:1, :], in_=src.rearrange("h d -> (h d)").unsqueeze(0)),
            w=[row.name], dma=True)
    P.op("gpsimd", lambda e: e.dma_start(out=BO[0:1, :], in_=bo_h.unsqueeze(0)), w=["BO"], dma=True)
    P.op("gpsimd", lambda e: e.dma_start(out=BF_[0:1, :], in_=bf_h.unsqueeze(0)), w=["BF_"], dma=True)
    P.op("gpsimd", lambda e: e.memset(MASKT[:, :], 0.0), w=["MASKT"])
    P.op("gpsimd", lambda e: e.affine_select(out=MASKT[:, :], in_=MASKT[:, :], compare_op=ALU.is_ge,
                                             fill=NEG, base=0, pattern=[[1, 128]],
                                             channel_multiplier=-1),
         r=["MASKT"], w=["MASKT"])

    for oc in range(16):
        pp = oc % 2
        xt, gt = XT[pp], GT[pp]
        P.op("sync", lambda e, oc=oc, xt=xt: e.dma_start(
            out=xt[:, :], in_=wf_h[oc * 128:(oc + 1) * 128, :]), w=[f"xt{pp}"], dma=True)
        P.op("vector", lambda e, xt=xt, pp=pp: e.reduce_sum(
            cx.tmp1[pp][:, :], xt[:, :], axis=AX.X, apply_absolute_value=True),
            r=[f"xt{pp}"], w=[f"tmp1{pp}"])
        P.op("vector", lambda e, pp=pp: e.tensor_scalar(
            cx.tmp1[pp][:, :], cx.tmp1[pp][:, :], 1.0 / C, None, op0=ALU.mult),
            r=[f"tmp1{pp}"], w=[f"tmp1{pp}"])
        P.op("scalar", lambda e, xt=xt: e.sign(XNT[:, :], xt[:, :]), r=[f"xt{pp}"], w=["XNT"])
        P.op("vector", lambda e, gt=gt, pp=pp: e.tensor_scalar(
            gt[:, :], XNT[:, :], cx.tmp1[pp][:, :], None, op0=ALU.mult),
            r=["XNT", f"tmp1{pp}"], w=[f"gt{pp}"])
        P.op("gpsimd", lambda e, gt=gt, oc=oc: e.dma_start(
            out=bw_d[oc * 128:(oc + 1) * 128, :], in_=gt[:, :]),
            r=[f"gt{pp}"], w=[f"bwd.{oc}"], dma=True)
    bwd_all = [f"bwd.{oc}" for oc in range(16)]

    for b in range(2):
        for i in range(16):
            pp = i % 2
            xt = XT[pp]
            P.op("sync", lambda e, b=b, i=i, xt=xt: e.dma_start(
                out=xt[:, :], in_=x_h[b, i * 128:(i + 1) * 128, :]), w=[f"xt{pp}"], dma=True)
            ln_tile(P, cx, xt, f"xt{pp}", XNT, "XNT", pp)
            P.op("gpsimd", lambda e, i=i: e.dma_start(
                out=xn_d[i * 128:(i + 1) * 128, :], in_=XNT[:, :]),
                r=["XNT"], w=[f"xnd.{i}"], dma=True)
        xnd_all = [f"xnd.{i}" for i in range(16)]
        for fc in range(FC):
            P.op("scalar", lambda e, fc=fc: e.dma_start(
                out=XNFM[:, fc * 2048:(fc + 1) * 2048],
                in_=xn_d[:, fc * 128:(fc + 1) * 128], transpose=True),
                r=xnd_all, w=[f"XNFM#{fc}"], dma=True)

        for half in range(2):
            for ic in range(FC):
                c0 = ic * 1024
                P.op("scalar", lambda e, ic=ic, half=half, c0=c0: e.dma_start(
                    out=PT[:, c0:c0 + 1024],
                    in_=bw_d[half * 1024:(half + 1) * 1024, ic * 128:(ic + 1) * 128],
                    transpose=True),
                    r=bwd_all, w=keys2k("PT", c0, c0 + 1024), dma=True)
            for i in range(16):
                pp = i % 2
                gt = GT[pp]
                for cg in range(2):
                    grp, gk = cx.rot()
                    ocol = half * 1024 + cg * 512
                    for ic in range(FC):
                        P.op("tensor", lambda e, grp=grp, ic=ic, i=i, cg=cg: e.matmul(
                            grp[:, :], XNFM[:, ic * 2048 + i * 128: ic * 2048 + (i + 1) * 128],
                            PT[:, ic * 1024 + cg * 512: ic * 1024 + (cg + 1) * 512],
                            start=(ic == 0), stop=False),
                            r=[f"XNFM#{ic}"] + keys2k("PT", ic * 1024, ic * 1024 + 1024), w=[gk])
                    P.op("tensor", lambda e, grp=grp, ocol=ocol: e.matmul(
                        grp[:, :], cx.ones_r[0:1, 0:128], BF_[0:1, ocol:ocol + 512],
                        start=False, stop=True),
                        r=["ones_r", "BF_"], w=[gk])
                    P.op("scalar", lambda e, grp=grp, gt=gt, cg=cg: e.activation(
                        gt[:, cg * 512:(cg + 1) * 512], grp[:, :], AF.Silu),
                        r=[gk], w=[f"gt{pp}"])
                P.op("gpsimd", lambda e, gt=gt, i=i, half=half: e.dma_start(
                    out=g_d[i * 128:(i + 1) * 128, half * 1024:(half + 1) * 1024],
                    in_=gt[:, 0:1024]),
                    r=[f"gt{pp}"], w=[f"gd.{i}.{half}"], dma=True)

        for hgv in range(4):
            for ic in range(FC):
                c0 = ic * 512
                P.op("gpsimd", lambda e, hgv=hgv, ic=ic, c0=c0: e.dma_start(
                    out=PT[:, c0:c0 + 512],
                    in_=wv_h.rearrange("h i d -> i h d")[ic * 128:(ic + 1) * 128,
                                                         4 * hgv:4 * hgv + 4, :]),
                    w=keys2k("PT", c0, c0 + 512), dma=True)
            for i in range(16):
                grp, gk = cx.rot()
                for ic in range(FC):
                    P.op("tensor", lambda e, grp=grp, ic=ic, i=i: e.matmul(
                        grp[:, :], XNFM[:, ic * 2048 + i * 128: ic * 2048 + (i + 1) * 128],
                        PT[:, ic * 512:(ic + 1) * 512], start=(ic == 0), stop=False),
                        r=[f"XNFM#{ic}"] + keys2k("PT", ic * 512, ic * 512 + 512), w=[gk])
                P.op("tensor", lambda e, grp=grp, hgv=hgv: e.matmul(
                    grp[:, :], cx.ones_r[0:1, 0:128], BV[0:1, hgv * 512:(hgv + 1) * 512],
                    start=False, stop=True),
                    r=["ones_r", "BV"], w=[gk])
                P.op("scalar", lambda e, grp=grp: e.activation(VT[:, :], grp[:, :], AF.Copy),
                     r=[gk], w=["VT"])
                P.op("gpsimd", lambda e, i=i, hgv=hgv: e.dma_start(
                    out=v_d.rearrange("h t d -> t h d")[i * 128:(i + 1) * 128,
                                                        4 * hgv:4 * hgv + 4, :],
                    in_=VT[:, :]),
                    r=["VT"], w=[f"vd.{hgv}.{i}"], dma=True)

        for h in range(H):
            pp = h % 2
            wq, wk = WQH[pp], WKH[pp]
            P.op("gpsimd", lambda e, h=h, wq=wq: e.dma_start(
                out=wq[:, :], in_=wq_h[h].rearrange("(fc f) d -> f fc d", f=128)),
                w=[f"wqh{pp}"], dma=True)
            P.op("gpsimd", lambda e, h=h, wk=wk: e.dma_start(
                out=wk[:, :], in_=wk_h[h].rearrange("(fc f) d -> f fc d", f=128)),
                w=[f"wkh{pp}"], dma=True)
            P.op("sync", lambda e, h=h: e.dma_start(
                out=VH[:, :].rearrange("t (c d) -> t c d", d=128),
                in_=v_d[h].rearrange("(c t) d -> t c d", t=128)),
                r=[f"vd.{h // 4}.{i}" for i in range(16)], w=["VH"], dma=True)
            for dst, wgt, wkey, brow, bkey, dkey, scl in (
                    (QFM, wq, f"wqh{pp}", BQ, "BQ", "QFM", SCALE),
                    (KFM, wk, f"wkh{pp}", BK, "BK", "KFM", None)):
                for tg in range(4):
                    grp, gk = cx.rot()
                    for fc in range(FC):
                        P.op("tensor", lambda e, grp=grp, wgt=wgt, fc=fc, tg=tg: e.matmul(
                            grp[:, :], wgt[:, fc * 128:(fc + 1) * 128],
                            XNFM[:, fc * 2048 + tg * 512: fc * 2048 + (tg + 1) * 512],
                            start=(fc == 0), stop=False),
                            r=[wkey, f"XNFM#{fc}"], w=[gk])
                    P.op("tensor", lambda e, grp=grp, brow=brow, h=h: e.matmul(
                        grp[:, :], brow[0:1, h * 128:(h + 1) * 128], cx.ones_r[0:1, 0:512],
                        start=False, stop=True),
                        r=[bkey, "ones_r"], w=[gk])
                    if scl is not None:
                        P.op("scalar", lambda e, grp=grp, dst=dst, tg=tg, scl=scl: e.activation(
                            dst[:, tg * 512:(tg + 1) * 512], grp[:, :], AF.Copy, scale=scl),
                            r=[gk], w=[dkey])
                    else:
                        P.op("scalar", lambda e, grp=grp, dst=dst, tg=tg: e.activation(
                            dst[:, tg * 512:(tg + 1) * 512], grp[:, :], AF.Copy),
                            r=[gk], w=[dkey])
            for c in range(16):
                Lc = T - c * 128
                for g in range((Lc + 511) // 512):
                    N = min(512, Lc - g * 512)
                    grp, gk = cx.rot()
                    P.op("tensor", lambda e, grp=grp, c=c, g=g, N=N: e.matmul(
                        grp[:, 0:N], KFM[:, c * 128:(c + 1) * 128],
                        QFM[:, c * 128 + g * 512: c * 128 + g * 512 + N],
                        start=True, stop=True),
                        r=["KFM", "QFM"], w=[gk])
                    if g == 0:
                        P.op("vector", lambda e, grp=grp: e.tensor_tensor(
                            grp[:, 0:128], grp[:, 0:128], MASKT[:, :], op=ALU.add),
                            r=[gk, "MASKT"], w=[gk])
                    p0 = PTOFF[c] + g * 512
                    P.op("scalar", lambda e, grp=grp, p0=p0, N=N: e.activation(
                        PT[:, p0:p0 + N], grp[:, 0:N], AF.Exp),
                        r=[gk], w=keys2k("PT", p0, p0 + N))
            for tg in range(4):
                t0 = tg * 512
                cmax = min(15, (t0 + 511) // 128)
                for c in range(cmax + 1):
                    lo = max(t0, c * 128)
                    N = t0 + 512 - lo
                    src = PTOFF[c] + lo - c * 128
                    P.op("tensor", lambda e, lo=lo, t0=t0, N=N, src=src, c=c, cmax=cmax: e.matmul(
                        cx.psl[0:1, lo - t0:lo - t0 + N], cx.ones_c[:, :], PT[:, src:src + N],
                        start=(c == 0), stop=(c == cmax)),
                        r=["ones_c"] + keys2k("PT", src, src + N), w=["PSL"])
                P.op("scalar", lambda e, tg=tg: e.activation(
                    LNL[0:1, tg * 512:(tg + 1) * 512], cx.psl[0:1, :], AF.Ln),
                    r=["PSL"], w=["LNL"])
            for c in range(16):
                for tg in range((c * 128) // 512, 4):
                    t0 = tg * 512
                    lo = max(t0, c * 128)
                    N = t0 + 512 - lo
                    src = PTOFF[c] + lo - c * 128
                    cmax = tg * 4 + 3
                    P.op("tensor", lambda e, c=c, tg=tg, lo=lo, t0=t0, N=N, src=src, cmax=cmax:
                         e.matmul(cx.ps[tg][:, lo - t0:lo - t0 + N], VH[:, c * 128:(c + 1) * 128],
                                  PT[:, src:src + N], start=(c == 0), stop=(c == cmax)),
                         r=["VH"] + keys2k("PT", src, src + N), w=[f"PS.{tg}"])
            for tg in range(4):
                gi = 4 + (tg % 3)
                grp = cx.ps[gi]
                gk = f"PS.{gi}"
                P.op("tensor", lambda e, grp=grp, tg=tg: e.matmul(
                    grp[:, :], cx.ones_r[0:1, 0:128], LNL[0:1, tg * 512:(tg + 1) * 512],
                    start=True, stop=True),
                    r=["ones_r", "LNL"], w=[gk])
                P.op("scalar", lambda e, grp=grp, tg=tg: e.activation(
                    RB[:, tg * 512:(tg + 1) * 512], grp[:, :], AF.Exp, scale=-1.0),
                    r=[gk], w=["RB"])
                P.op("vector", lambda e, tg=tg: e.tensor_tensor(
                    YH[:, tg * 512:(tg + 1) * 512], cx.ps[tg][:, :],
                    RB[:, tg * 512:(tg + 1) * 512], op=ALU.mult),
                    r=[f"PS.{tg}", "RB"], w=["YH"])
            P.op("gpsimd", lambda e, h=h: e.dma_start(out=y_d[h], in_=YH[:, :]),
                 r=["YH"], w=[f"yd.{h}"], dma=True)

        for fc in range(FC):
            P.op("gpsimd", lambda e, fc=fc: e.dma_start(
                out=XNFM[:, fc * 2048:(fc + 1) * 2048], in_=wo_h[fc * 128:(fc + 1) * 128, :]),
                w=[f"XNFM#{fc}"], dma=True)
        yd_all = [f"yd.{h}" for h in range(H)]
        for i in range(16):
            pp = i % 2
            xt, gt = XT[pp], GT[pp]
            yt = XNT if pp == 0 else WVC
            ytk = "XNT" if pp == 0 else "WVC"
            P.op("sync", lambda e, b=b, i=i, xt=xt: e.dma_start(
                out=xt[:, :], in_=x_h[b, i * 128:(i + 1) * 128, :]), w=[f"xt{pp}"], dma=True)
            P.op("sync", lambda e, i=i, yt=yt: e.dma_start(
                out=yt[:, :].rearrange("d (h t) -> d h t", h=H),
                in_=y_d.rearrange("h d t -> d h t")[:, :, i * 128:(i + 1) * 128]),
                r=yd_all, w=[ytk], dma=True)
            P.op("gpsimd", lambda e, i=i, gt=gt: e.dma_start(
                out=gt[:, :], in_=g_d[i * 128:(i + 1) * 128, :]),
                r=[f"gd.{i}.0", f"gd.{i}.1"], w=[f"gt{pp}"], dma=True)
            for cg in range(4):
                grp, gk = cx.rot()
                for fc in range(FC):
                    P.op("tensor", lambda e, grp=grp, yt=yt, fc=fc, cg=cg: e.matmul(
                        grp[:, :], yt[:, fc * 128:(fc + 1) * 128],
                        XNFM[:, fc * 2048 + cg * 512: fc * 2048 + (cg + 1) * 512],
                        start=(fc == 0), stop=False),
                        r=[ytk, f"XNFM#{fc}"], w=[gk])
                P.op("tensor", lambda e, grp=grp, cg=cg: e.matmul(
                    grp[:, :], cx.ones_r[0:1, 0:128], BO[0:1, cg * 512:(cg + 1) * 512],
                    start=False, stop=True),
                    r=["ones_r", "BO"], w=[gk])
                P.op("vector", lambda e, grp=grp, gt=gt, cg=cg: e.scalar_tensor_tensor(
                    SCR[:, :], grp[:, :], 1.0, gt[:, cg * 512:(cg + 1) * 512],
                    op0=ALU.mult, op1=ALU.mult),
                    r=[gk, f"gt{pp}"], w=["SCR"])
                P.op("vector", lambda e, xt=xt, cg=cg: e.tensor_tensor(
                    xt[:, cg * 512:(cg + 1) * 512], xt[:, cg * 512:(cg + 1) * 512], SCR[:, :],
                    op=ALU.add),
                    r=[f"xt{pp}", "SCR"], w=[f"xt{pp}"])
            P.op("sync", lambda e, b=b, i=i, xt=xt: e.dma_start(
                out=x1_h[b, i * 128:(i + 1) * 128, :], in_=xt[:, :]),
                r=[f"xt{pp}"], w=[f"x1.{b}.{i}"], dma=True)

    if debug:
        dbg_xn = nc.dram_tensor("dbg_xn", [T, C], BF16, kind="ExternalOutput").ap()
        dbg_g = nc.dram_tensor("dbg_g", [T, C], BF16, kind="ExternalOutput").ap()
        dbg_v = nc.dram_tensor("dbg_v", [H, T, DH], BF16, kind="ExternalOutput").ap()
        dbg_y = nc.dram_tensor("dbg_y", [H, DH, T], BF16, kind="ExternalOutput").ap()
        dbg_bw = nc.dram_tensor("dbg_bw", [C, C], BF16, kind="ExternalOutput").ap()
        P.op("gpsimd", lambda e: e.dma_start(out=dbg_xn, in_=xn_d),
             r=[f"xnd.{i}" for i in range(16)], dma=True)
        P.op("gpsimd", lambda e: e.dma_start(out=dbg_g, in_=g_d),
             r=[f"gd.{i}.{hf}" for i in range(16) for hf in range(2)], dma=True)
        P.op("gpsimd", lambda e: e.dma_start(out=dbg_v, in_=v_d),
             r=[f"vd.{hgv}.{i}" for hgv in range(4) for i in range(16)], dma=True)
        P.op("gpsimd", lambda e: e.dma_start(out=dbg_y, in_=y_d),
             r=[f"yd.{h}" for h in range(H)], dma=True)
        P.op("gpsimd", lambda e: e.dma_start(out=dbg_bw, in_=bw_d),
             r=[f"bwd.{oc}" for oc in range(16)], dma=True)
    return P, es


def build_a_program(debug=False):
    nc = bass.Bass("TRN2", target_bir_lowering=False)
    P, es = build_stage_a(nc, debug=debug)
    with es:
        P.build(es)
    return nc


def build_stage_b(nc):
    es = contextlib.ExitStack()
    cx = Ctx()
    x1_h = nc.dram_tensor("x1c", [B, NB, C], F32, kind="ExternalInput").ap()
    wqkv_h = nc.dram_tensor("wqkv", [3 * C, C], F32, kind="ExternalInput").ap()
    wout_h = nc.dram_tensor("wout", [C, C], F32, kind="ExternalInput").ap()
    w1_h = nc.dram_tensor("w1", [FH, C], F32, kind="ExternalInput").ap()
    w2_h = nc.dram_tensor("w2", [FH, C], F32, kind="ExternalInput").ap()
    w3_h = nc.dram_tensor("w3", [C, FH], F32, kind="ExternalInput").ap()
    out_h = nc.dram_tensor("outc", [B, NB, C], F32, kind="ExternalOutput").ap()

    wqkv_bf = nc.dram_tensor("wqkv_bf", [3 * C, C], BF16).ap()
    wout_bf = nc.dram_tensor("wout_bf", [C, C], BF16).ap()
    q1_d = nc.dram_tensor("q1_d", [FH, C], BF16).ap()
    q2_d = nc.dram_tensor("q2_d", [FH, C], BF16).ap()
    q3_d = nc.dram_tensor("q3_d", [C, FH], BF16).ap()
    q1t_d = nc.dram_tensor("q1t_d", [C, FH], BF16).ap()
    q2t_d = nc.dram_tensor("q2t_d", [C, FH], BF16).ap()
    q3t_d = nc.dram_tensor("q3t_d", [FH, C], BF16).ap()
    xn_d = nc.dram_tensor("xn_d", [2, 2048, C], BF16).ap()
    y1_d = nc.dram_tensor("y1_d", [2, 2048, C], BF16).ap()
    s1_d = nc.dram_tensor("s1_d", [2, 2048, C], BF16).ap()
    gfm_d = nc.dram_tensor("gfm_d", [C, 1024], BF16).ap()

    alloc_common(nc, es, cx)
    sb = lambda n, sh, dt: es.enter_context(nc.sbuf_tensor(n, sh, dt))
    G1 = sb("G1", [128, 32768], BF16)
    G2 = sb("G2", [128, 32768], BF16)
    G3 = sb("G3", [128, 16384], BF16)
    XT = [sb(f"XT{i}", [128, C], F32) for i in range(2)]
    XNT = sb("XNT", [128, C], BF16)
    SH = sb("SH", [128, 256], F32)
    PH = sb("PH", [128, 256], F32)
    PN = [sb(f"PN{i}", [128, 256], F32) for i in range(4)]
    MX = sb("MX", [128, 16], F32)
    SM = sb("SM", [128, 16], F32)
    SCRB = sb("SCRB", [128, 128], BF16)
    OB = sb("OB", [128, 512], F32)
    OBB = sb("OBB", [128, 512], BF16)
    S1T = sb("S1T", [128, 512], BF16)
    GFT = sb("GFT", [128, 1024], BF16)
    SGM = sb("SGM", [128, 1024], F32)
    SCR = sb("SCR", [128, 2048], F32)
    GRW = sb("GRW", [128, 1], F32)
    INV = sb("INV", [128, 1], F32)
    T11 = sb("T11", [1, 1], F32)

    P = Plan(nc)
    prologue_consts(P, cx)

    for j in range(12):
        P.op("gpsimd", lambda e, j=j: e.dma_start(
            out=wqkv_bf[j * 512:(j + 1) * 512, :], in_=wqkv_h[j * 512:(j + 1) * 512, :]),
            w=[f"wqkvbf.{j}"], dma=True)
    for j in range(4):
        P.op("gpsimd", lambda e, j=j: e.dma_start(
            out=wout_bf[j * 512:(j + 1) * 512, :], in_=wout_h[j * 512:(j + 1) * 512, :]),
            w=[f"woutbf.{j}"], dma=True)

    def quantize(w_hbm, q_d, rows, cols, tagw):
        nchunk = rows // 128
        nparts = (cols + 2047) // 2048
        P.op("gpsimd", lambda e: e.memset(GRW[:, :], 0.0), w=["GRW"])
        for ch in range(nchunk):
            pp = ch % 2
            xt = XT[pp]
            for part in range(nparts):
                c0, c1 = part * 2048, min(cols, (part + 1) * 2048)
                P.op("sync", lambda e, ch=ch, xt=xt, c0=c0, c1=c1: e.dma_start(
                    out=xt[:, 0:c1 - c0], in_=w_hbm[ch * 128:(ch + 1) * 128, c0:c1]),
                    w=[f"xt{pp}"], dma=True)
                P.op("vector", lambda e, xt=xt, pp=pp, n=c1 - c0: e.reduce_sum(
                    cx.tmp1[pp][:, :], xt[:, 0:n], axis=AX.X, apply_absolute_value=True),
                    r=[f"xt{pp}"], w=[f"tmp1{pp}"])
                P.op("vector", lambda e, pp=pp: e.tensor_tensor(
                    GRW[:, :], GRW[:, :], cx.tmp1[pp][:, :], op=ALU.add),
                    r=["GRW", f"tmp1{pp}"], w=["GRW"])
        P.op("tensor", lambda e: e.matmul(cx.psl[0:1, 0:1], GRW[:, :], cx.ones_f[:, :],
                                          start=True, stop=True),
             r=["GRW", "ones_f"], w=["PSL"])
        P.op("vector", lambda e, rows=rows, cols=cols: e.tensor_scalar(
            T11[:, :], cx.psl[0:1, 0:1], 1.0 / (rows * cols), 1e-8, op0=ALU.mult, op1=ALU.add),
            r=["PSL"], w=["T11"])
        P.op("vector", lambda e: e.reciprocal(T11[:, :], T11[:, :]), r=["T11"], w=["T11"])
        grp, gk = cx.rot()
        P.op("tensor", lambda e, grp=grp: e.matmul(grp[:, 0:1], cx.ones_fr[:, :], T11[:, :],
                                                   start=True, stop=True),
             r=["ones_fr", "T11"], w=[gk])
        P.op("vector", lambda e, grp=grp: e.tensor_copy(INV[:, :], grp[:, 0:1]), r=[gk], w=["INV"])
        for ch in range(nchunk):
            pp = ch % 2
            xt = XT[pp]
            for part in range(nparts):
                c0, c1 = part * 2048, min(cols, (part + 1) * 2048)
                n = c1 - c0
                P.op("sync", lambda e, ch=ch, xt=xt, c0=c0, c1=c1: e.dma_start(
                    out=xt[:, 0:c1 - c0], in_=w_hbm[ch * 128:(ch + 1) * 128, c0:c1]),
                    w=[f"xt{pp}"], dma=True)
                P.op("scalar", lambda e, xt=xt, n=n: e.activation(XNT[:, 0:n], xt[:, 0:n], AF.Sign),
                     r=[f"xt{pp}"], w=["XNT"])
                P.op("vector", lambda e, xt=xt, n=n: e.tensor_scalar(
                    xt[:, 0:n], xt[:, 0:n], INV[:, :], None, op0=ALU.mult),
                    r=[f"xt{pp}", "INV"], w=[f"xt{pp}"])
                P.op("scalar", lambda e, xt=xt, n=n: e.activation(SCR[:, 0:n], xt[:, 0:n], AF.Square),
                     r=[f"xt{pp}"], w=["SCR"])
                P.op("vector", lambda e, n=n: e.scalar_tensor_tensor(
                    XNT[:, 0:n], SCR[:, 0:n], 0.25, XNT[:, 0:n], op0=ALU.is_gt, op1=ALU.mult),
                    r=["SCR", "XNT"], w=["XNT"])
                P.op("gpsimd", lambda e, ch=ch, c0=c0, c1=c1: e.dma_start(
                    out=q_d[ch * 128:(ch + 1) * 128, c0:c1], in_=XNT[:, 0:c1 - c0]),
                    r=["XNT"], w=[f"{tagw}.{ch}.{c0}"], dma=True)

    quantize(w1_h, q1_d, FH, C, "q1")
    quantize(w2_h, q2_d, FH, C, "q2")
    quantize(w3_h, q3_d, C, FH, "q3")
    q1keys = [f"q1.{ch}.0" for ch in range(HC)]
    q2keys = [f"q2.{ch}.0" for ch in range(HC)]
    q3keys = [f"q3.{ch}.{c0}" for ch in range(FC) for c0 in range(0, FH, 2048)]

    for (qd, qtd, rk, tag) in ((q1_d, q1t_d, q1keys, "q1t"), (q2_d, q2t_d, q2keys, "q2t")):
        for ec in range(FC):
            P.op("scalar", lambda e, qd=qd, ec=ec: e.dma_start(
                out=G1[:, 0:FH], in_=qd[:, ec * 128:(ec + 1) * 128], transpose=True),
                r=rk, w=keys2k("G1", 0, FH), dma=True)
            P.op("gpsimd", lambda e, qtd=qtd, ec=ec: e.dma_start(
                out=qtd[ec * 128:(ec + 1) * 128, :], in_=G1[:, 0:FH]),
                r=keys2k("G1", 0, FH), w=[f"{tag}.{ec}"], dma=True)
    for hc in range(HC):
        P.op("scalar", lambda e, hc=hc: e.dma_start(
            out=G1[:, 0:C], in_=q3_d[:, hc * 128:(hc + 1) * 128], transpose=True),
            r=q3keys, w=keys2k("G1", 0, C), dma=True)
        P.op("gpsimd", lambda e, hc=hc: e.dma_start(
            out=q3t_d[hc * 128:(hc + 1) * 128, :], in_=G1[:, 0:C]),
            r=keys2k("G1", 0, C), w=[f"q3t.{hc}"], dma=True)
    q1t_all = [f"q1t.{ec}" for ec in range(FC)]
    q2t_all = [f"q2t.{ec}" for ec in range(FC)]
    q3t_all = [f"q3t.{hc}" for hc in range(HC)]

    for ncx in range(2):
        for l in range(16):
            pp = l % 2
            xt = XT[pp]
            P.op("sync", lambda e, l=l, ncx=ncx, xt=xt: e.dma_start(
                out=xt[:, :], in_=x1_h[l, ncx * 128:(ncx + 1) * 128, :]),
                w=[f"xt{pp}"], dma=True)
            ln_tile(P, cx, xt, f"xt{pp}", XNT, "XNT", pp)
            P.op("gpsimd", lambda e, l=l, ncx=ncx: e.dma_start(
                out=xn_d[ncx, l * 128:(l + 1) * 128, :], in_=XNT[:, :]),
                r=["XNT"], w=[f"xnd.{ncx}.{l}"], dma=True)
        xnd_nc = [f"xnd.{ncx}.{l}" for l in range(16)]
        for ec in range(FC):
            P.op("scalar", lambda e, ec=ec, ncx=ncx: e.dma_start(
                out=G1[:, ec * 2048:(ec + 1) * 2048],
                in_=xn_d[ncx][:, ec * 128:(ec + 1) * 128], transpose=True),
                r=xnd_nc, w=[f"G1#{ec}"], dma=True)
        for hg in range(4):
            bases = [hg * 512, 2048 + hg * 512, 4096 + hg * 512]
            for part, base in enumerate(bases):
                for ec in range(FC):
                    c0 = 24576 + ec * 512
                    P.op("scalar", lambda e, base=base, ec=ec, c0=c0: e.dma_start(
                        out=G2[:, c0:c0 + 512],
                        in_=wqkv_bf[base:base + 512, ec * 128:(ec + 1) * 128], transpose=True),
                        r=[f"wqkvbf.{base // 512}"], w=keys2k("G2", c0, c0 + 512), dma=True)
                for l in range(16):
                    grp, gk = cx.rot()
                    for ec in range(FC):
                        P.op("tensor", lambda e, grp=grp, ec=ec, l=l: e.matmul(
                            grp[:, :], G1[:, ec * 2048 + l * 128: ec * 2048 + (l + 1) * 128],
                            G2[:, 24576 + ec * 512: 24576 + (ec + 1) * 512],
                            start=(ec == 0), stop=(ec == FC - 1)),
                            r=[f"G1#{ec}"] + keys2k("G2", 24576 + ec * 512, 25088 + ec * 512),
                            w=[gk])
                    d0 = part * 8192 + l * 512
                    P.op("scalar", lambda e, grp=grp, d0=d0: e.activation(
                        G2[:, d0:d0 + 512], grp[:, :], AF.Copy),
                        r=[gk], w=keys2k("G2", d0, d0 + 512))
            qkkeys = keys2k("G2", 0, 16384)
            vkeys = keys2k("G2", 16384, 24576)
            for hl in range(4):
                P.op("gpsimd", lambda e: e.memset(SH[:, :], NEG), w=["SH"])
                for l in range(16):
                    qc = l * 512 + hl * 128
                    for m in range(l + 1):
                        kc = 8192 + m * 512 + hl * 128
                        P.op("vector", lambda e, qc=qc, kc=kc, l=l, m=m: e.scalar_tensor_tensor(
                            SCRB[:, :], G2[:, qc:qc + 128], SCALE, G2[:, kc:kc + 128],
                            op0=ALU.mult, op1=ALU.mult,
                            accum_out=SH[:, l * 16 + m: l * 16 + m + 1]),
                            r=qkkeys, w=["SCRB", "SH"])
                P.op("vector", lambda e: e.reduce_max(
                    MX[:, :], SH[:, :].rearrange("p (l m) -> p l m", m=16), axis=AX.X),
                    r=["SH"], w=["MX"])
                P.op("vector", lambda e: e.tensor_tensor(
                    PH[:, :].rearrange("p (l m) -> p l m", m=16),
                    SH[:, :].rearrange("p (l m) -> p l m", m=16),
                    MX[:, :].unsqueeze(2).broadcast_to([128, 16, 16]), op=ALU.subtract),
                    r=["SH", "MX"], w=["PH"])
                P.op("scalar", lambda e: e.activation(PH[:, :], PH[:, :], AF.Exp),
                     r=["PH"], w=["PH"])
                P.op("vector", lambda e: e.reduce_sum(
                    SM[:, :], PH[:, :].rearrange("p (l m) -> p l m", m=16), axis=AX.X),
                    r=["PH"], w=["SM"])
                P.op("vector", lambda e: e.reciprocal(SM[:, :], SM[:, :]), r=["SM"], w=["SM"])
                pn = PN[hl]
                P.op("vector", lambda e, pn=pn: e.tensor_tensor(
                    pn[:, :].rearrange("p (l m) -> p l m", m=16),
                    PH[:, :].rearrange("p (l m) -> p l m", m=16),
                    SM[:, :].unsqueeze(2).broadcast_to([128, 16, 16]), op=ALU.mult),
                    r=["PH", "SM"], w=[f"PN{hl}"])
            for l in range(16):
                for hl in range(4):
                    pn = PN[hl]
                    for m in range(l + 1):
                        vc = 16384 + m * 512 + hl * 128
                        if m == 0:
                            P.op("vector", lambda e, vc=vc, hl=hl, pn=pn, l=l: e.tensor_scalar(
                                OB[:, hl * 128:(hl + 1) * 128], G2[:, vc:vc + 128],
                                pn[:, l * 16: l * 16 + 1], None, op0=ALU.mult),
                                r=vkeys + [f"PN{hl}"], w=["OB"])
                        else:
                            P.op("vector", lambda e, vc=vc, hl=hl, pn=pn, l=l, m=m:
                                 e.scalar_tensor_tensor(
                                     OB[:, hl * 128:(hl + 1) * 128], G2[:, vc:vc + 128],
                                     pn[:, l * 16 + m: l * 16 + m + 1],
                                     OB[:, hl * 128:(hl + 1) * 128],
                                     op0=ALU.mult, op1=ALU.add),
                                 r=vkeys + [f"PN{hl}", "OB"], w=["OB"])
                P.op("vector", lambda e: e.tensor_copy(OBB[:, :], OB[:, :]), r=["OB"], w=["OBB"])
                P.op("gpsimd", lambda e, ncx=ncx, l=l, hg=hg: e.dma_start(
                    out=y1_d[ncx, l * 128:(l + 1) * 128, hg * 512:(hg + 1) * 512], in_=OBB[:, :]),
                    r=["OBB"], w=[f"y1.{ncx}.{l}.{hg}"], dma=True)
        y1keys = [f"y1.{ncx}.{l}.{hg}" for l in range(16) for hg in range(4)]
        for ec in range(FC):
            P.op("scalar", lambda e, ec=ec, ncx=ncx: e.dma_start(
                out=G1[:, ec * 2048:(ec + 1) * 2048],
                in_=y1_d[ncx][:, ec * 128:(ec + 1) * 128], transpose=True),
                r=y1keys, w=[f"G1#{ec}"], dma=True)
        for cg in range(4):
            for ec in range(FC):
                c0 = 24576 + ec * 512
                P.op("scalar", lambda e, cg=cg, ec=ec, c0=c0: e.dma_start(
                    out=G2[:, c0:c0 + 512],
                    in_=wout_bf[cg * 512:(cg + 1) * 512, ec * 128:(ec + 1) * 128], transpose=True),
                    r=[f"woutbf.{cg}"], w=keys2k("G2", c0, c0 + 512), dma=True)
            for l in range(16):
                grp, gk = cx.rot()
                for ec in range(FC):
                    P.op("tensor", lambda e, grp=grp, ec=ec, l=l: e.matmul(
                        grp[:, :], G1[:, ec * 2048 + l * 128: ec * 2048 + (l + 1) * 128],
                        G2[:, 24576 + ec * 512: 24576 + (ec + 1) * 512],
                        start=(ec == 0), stop=(ec == FC - 1)),
                        r=[f"G1#{ec}"] + keys2k("G2", 24576 + ec * 512, 25088 + ec * 512),
                        w=[gk])
                P.op("scalar", lambda e, grp=grp: e.activation(S1T[:, :], grp[:, :], AF.Copy),
                     r=[gk], w=["S1T"])
                P.op("gpsimd", lambda e, ncx=ncx, l=l, cg=cg: e.dma_start(
                    out=s1_d[ncx, l * 128:(l + 1) * 128, cg * 512:(cg + 1) * 512], in_=S1T[:, :]),
                    r=["S1T"], w=[f"s1.{ncx}.{l}.{cg}"], dma=True)

    for rg in range(4):
        l0 = rg * 4
        xnd_rg = [f"xnd.{ncx}.{l}" for ncx in range(2) for l in range(l0, l0 + 4)]
        for ec in range(FC):
            for ncx in range(2):
                c0 = 12288 + ec * 1024 + ncx * 512
                P.op("scalar", lambda e, ec=ec, ncx=ncx, l0=l0, c0=c0: e.dma_start(
                    out=G2[:, c0:c0 + 512],
                    in_=xn_d[ncx, l0 * 128:(l0 + 4) * 128, ec * 128:(ec + 1) * 128],
                    transpose=True),
                    r=xnd_rg, w=keys2k("G2", c0, c0 + 512), dma=True)
        for hc in range(HC):
            for (qtd, qk, c0) in ((q1t_d, q1t_all, 28672), (q2t_d, q2t_all, 30720)):
                P.op("sync", lambda e, qtd=qtd, hc=hc, c0=c0: e.dma_start(
                    out=G2[:, c0:c0 + 2048].rearrange("p (ec d) -> p ec d", d=128),
                    in_=qtd[:, hc * 128:(hc + 1) * 128].rearrange("(ec e) d -> e ec d", e=128)),
                    r=qk, w=keys2k("G2", c0, c0 + 2048), dma=True)
            pa = hc % 2
            gsets = ((cx.ps[pa * 2], f"PS.{pa * 2}"), (cx.ps[pa * 2 + 1], f"PS.{pa * 2 + 1}"))
            gsets2 = ((cx.ps[4 + pa], f"PS.{4 + pa}"), (cx.ps[6], "PS.6"))
            for (wc0, gs) in ((28672, gsets), (30720, gsets2)):
                for half, (gg, kk) in enumerate(gs):
                    for ec in range(FC):
                        P.op("tensor", lambda e, gg=gg, wc0=wc0, ec=ec, half=half: e.matmul(
                            gg[:, :], G2[:, wc0 + ec * 128: wc0 + (ec + 1) * 128],
                            G2[:, 12288 + ec * 1024 + half * 512: 12800 + ec * 1024 + half * 512],
                            start=(ec == 0), stop=(ec == FC - 1)),
                            r=keys2k("G2", wc0, wc0 + 2048) +
                              keys2k("G2", 12288 + ec * 1024, 12288 + (ec + 1) * 1024),
                            w=[kk])
            for half, (gg, kk) in enumerate(gsets):
                P.op("scalar", lambda e, gg=gg, half=half: e.activation(
                    SGM[:, half * 512:(half + 1) * 512], gg[:, :], AF.Sigmoid),
                    r=[kk], w=["SGM"])
            hcol = hc * 1024
            hidt = G1 if hcol + 1024 <= 32768 else G2
            hbase = hcol if hcol + 1024 <= 32768 else hcol - 32768
            hname = "G1" if hcol + 1024 <= 32768 else "G2"
            for half, (gg, kk) in enumerate(gsets2):
                P.op("vector", lambda e, hidt=hidt, hbase=hbase, half=half, gg=gg: e.tensor_tensor(
                    hidt[:, hbase + half * 512: hbase + (half + 1) * 512],
                    SGM[:, half * 512:(half + 1) * 512], gg[:, :], op=ALU.mult),
                    r=["SGM", kk], w=keys2k(hname, hbase + half * 512, hbase + (half + 1) * 512))
        for ec in range(FC):
            P.op("sync", lambda e, ec=ec: e.dma_start(
                out=G3[:, 0:FH].rearrange("p (hcc d) -> p hcc d", d=128),
                in_=q3t_d[:, ec * 128:(ec + 1) * 128].rearrange("(hcc h) d -> h hcc d", h=128)),
                r=q3t_all, w=keys2k("G3", 0, FH), dma=True)
            pa = ec % 2
            for half in range(2):
                gg = cx.ps[pa * 2 + half]
                kk = f"PS.{pa * 2 + half}"
                for hc in range(HC):
                    hcol = hc * 1024
                    hidt = G1 if hcol + 1024 <= 32768 else G2
                    hbase = hcol if hcol + 1024 <= 32768 else hcol - 32768
                    hname = "G1" if hcol + 1024 <= 32768 else "G2"
                    P.op("tensor", lambda e, gg=gg, hc=hc, hidt=hidt, hbase=hbase, half=half:
                         e.matmul(gg[:, :], G3[:, hc * 128:(hc + 1) * 128],
                                  hidt[:, hbase + half * 512: hbase + half * 512 + 512],
                                  start=(hc == 0), stop=(hc == HC - 1)),
                         r=keys2k("G3", hc * 128, (hc + 1) * 128) +
                           keys2k(hname, hbase, hbase + 1024), w=[kk])
                P.op("scalar", lambda e, gg=gg, half=half: e.activation(
                    GFT[:, half * 512:(half + 1) * 512], gg[:, :], AF.Silu),
                    r=[kk], w=["GFT"])
            P.op("gpsimd", lambda e, ec=ec: e.dma_start(
                out=gfm_d[ec * 128:(ec + 1) * 128, :], in_=GFT[:, :]),
                r=["GFT"], w=[f"gfm.{ec}"], dma=True)
        gfm_all = [f"gfm.{ec}" for ec in range(FC)]
        for rt in range(8):
            ncx = rt // 4
            l = l0 + rt % 4
            pp = rt % 2
            xt = XT[pp]
            P.op("sync", lambda e, l=l, ncx=ncx, xt=xt: e.dma_start(
                out=xt[:, :], in_=x1_h[l, ncx * 128:(ncx + 1) * 128, :]),
                w=[f"xt{pp}"], dma=True)
            P.op("scalar", lambda e, rt=rt: e.dma_start(
                out=XNT[:, :], in_=gfm_d[:, rt * 128:(rt + 1) * 128], transpose=True),
                r=gfm_all, w=["XNT"], dma=True)
            P.op("sync", lambda e, ncx=ncx, l=l: e.dma_start(
                out=G3[:, FH:FH + 2048], in_=s1_d[ncx, l * 128:(l + 1) * 128, :]),
                r=[f"s1.{ncx}.{l}.{cg}" for cg in range(4)],
                w=keys2k("G3", FH, FH + 2048), dma=True)
            for cg in range(4):
                P.op("vector", lambda e, cg=cg: e.tensor_tensor(
                    SCR[:, cg * 512:(cg + 1) * 512],
                    G3[:, FH + cg * 512:FH + (cg + 1) * 512],
                    XNT[:, cg * 512:(cg + 1) * 512], op=ALU.mult),
                    r=keys2k("G3", FH, FH + 2048) + ["XNT"], w=["SCR"])
                P.op("vector", lambda e, xt=xt, cg=cg: e.tensor_tensor(
                    xt[:, cg * 512:(cg + 1) * 512], xt[:, cg * 512:(cg + 1) * 512],
                    SCR[:, cg * 512:(cg + 1) * 512], op=ALU.add),
                    r=[f"xt{pp}", "SCR"], w=[f"xt{pp}"])
            P.op("sync", lambda e, l=l, ncx=ncx, xt=xt: e.dma_start(
                out=out_h[l, ncx * 128:(ncx + 1) * 128, :], in_=xt[:, :]),
                r=[f"xt{pp}"], w=[f"out.{l}.{ncx}"], dma=True)

    return P, es


def build_b_program():
    nc = bass.Bass("TRN2", target_bir_lowering=False)
    P, es = build_stage_b(nc)
    with es:
        P.build(es)
    return nc


def kernel(L0_attn_x, L0_attn_Wq, L0_attn_bq, L0_attn_Wk, L0_attn_bk, L0_attn_Wv,
           L0_attn_bv, L0_attn_Wo, L0_attn_bo, L0_ffn_weight, L0_ffn_bias,
           L1_attn_weight_qkv, L1_attn_weight_out, L1_ffn_w1, L1_ffn_w2, L1_ffn_w3):
    f = lambda a: np.ascontiguousarray(np.asarray(a, np.float32))
    x = f(L0_attn_x)
    wA = {"wq": f(L0_attn_Wq), "wk": f(L0_attn_Wk), "wv": f(L0_attn_Wv),
          "bq": f(L0_attn_bq), "bk": f(L0_attn_bk), "bv": f(L0_attn_bv),
          "wo": f(L0_attn_Wo), "bo": f(L0_attn_bo),
          "wf": f(L0_ffn_weight), "bf": f(L0_ffn_bias)}
    ncA = build_a_program()
    in_maps = [dict(wA, xa=np.ascontiguousarray(x[2 * c:2 * c + 2])) for c in range(NCORES)]
    resA = run_bass_kernel_spmd(ncA, in_maps, core_ids=list(range(NCORES)))
    x1 = np.concatenate([resA.results[c]["x1out"] for c in range(NCORES)], axis=0)

    wB = {"wqkv": f(L1_attn_weight_qkv), "wout": f(L1_attn_weight_out),
          "w1": f(L1_ffn_w1), "w2": f(L1_ffn_w2), "w3": f(L1_ffn_w3)}
    ncB = build_b_program()
    in_maps_b = [dict(wB, x1c=np.ascontiguousarray(x1[:, c * NB:(c + 1) * NB, :]))
                 for c in range(NCORES)]
    resB = run_bass_kernel_spmd(ncB, in_maps_b, core_ids=list(range(NCORES)))
    out = np.concatenate([resB.results[c]["outc"] for c in range(NCORES)], axis=1)
    return out
